# version 01 (final)
# baseline (speedup 1.0000x reference)
import numpy as np
from contextlib import ExitStack

import concourse.bass as bass
import concourse.mybir as mybir
from concourse.bass_utils import run_bass_kernel_spmd

F32 = mybir.dt.float32
BF16 = mybir.dt.bfloat16
FP8 = mybir.dt.float8e4
ALU = mybir.AluOpType
ACT = mybir.ActivationFunctionType
AXX = mybir.AxisListType.X

B, S, H = 8, 2048, 2048
NH = 16
II = 5632
NCH = 16
ICH = 44
NSL = 11

K23 = float(3 * 2**22)
K52 = float(3 * 2**51)
K42 = float(3 * 2**42)
SQ128 = float(np.float32(np.sqrt(np.float32(128.0))))
SHIFT = 12.0
INV2048 = float(2.0 ** -11)
BIGNEG = -3.0e38
BIGPOS = 3.0e38
PHASES = 9
P2N = 16
P2STOP = 9


class Arena:
    def __init__(self, ncols):
        self.ncols = ncols
        self.live = []

    def alloc(self, cols, e0, e1, name):
        self.live = [x for x in self.live if x[2] >= e0]
        taken = sorted((s, e) for (s, e, _, _) in self.live)
        pos = 0
        for (s, e) in taken:
            if pos + cols <= s:
                break
            pos = max(pos, e)
        if pos + cols > self.ncols:
            raise RuntimeError(f"arena overflow: {name} +{cols} at {pos}/{self.ncols}")
        self.live.append((pos, pos + cols, e1, name))
        return pos


class Em:

    def __init__(self, sems):
        self.sems = sems
        self.eng = None
        self.h = None
        self.cnt = None

    def start(self, eng, handle):
        self.eng = eng
        self.h = handle
        self.cnt = {k: 0 for k in self.sems}

    def _op(self, eng, fn, inc):
        val = None
        if inc is not None:
            self.cnt[inc] += 1
            val = self.cnt[inc]
        if self.eng == eng:
            ins = fn(self.h)
            if inc is not None:
                ins.then_inc(self.sems[inc], 1)
        return val

    def v(self, fn, inc=None):
        return self._op("vector", fn, inc)

    def s(self, fn, inc=None):
        return self._op("scalar", fn, inc)

    def t(self, fn, inc=None):
        return self._op("tensor", fn, inc)

    def _dma(self, eng, sem, fn):
        self.cnt[sem] += 16
        if self.eng == eng:
            fn(self.h).then_inc(self.sems[sem], 16)
        return self.cnt[sem]

    def dg(self, fn):
        return self._dma("gpsimd", "G", fn)

    def dl(self, fn):
        return self._dma("sync", "L", fn)

    def dst(self, fn):
        return self._dma("gpsimd", "St", fn)

    def w(self, eng, sem, val):
        if self.eng == eng and val:
            self.h.wait_ge(self.sems[sem], val)


def build_nc(dbg=False, phases=9, p2n=16, p2stop=9):
    global PHASES, P2N, P2STOP
    PHASES = phases
    P2N = p2n
    P2STOP = p2stop
    nc = bass.Bass("TRN2", target_bir_lowering=False)
    xT = nc.dram_tensor("xT", [H, S], F32, kind="ExternalInput").ap()
    wqkv = nc.dram_tensor("wqkv", [H, 3 * H], F32, kind="ExternalInput").ap()
    wout = nc.dram_tensor("wout", [H, H], F32, kind="ExternalInput").ap()
    boutT = nc.dram_tensor("boutT", [128, NCH], F32, kind="ExternalInput").ap()
    wguT = nc.dram_tensor("wguT", [H, 2 * II], F32, kind="ExternalInput").ap()
    wdnT = nc.dram_tensor("wdnT", [II, H], F32, kind="ExternalInput").ap()
    eye = nc.dram_tensor("eye", [128, 128], F32, kind="ExternalInput").ap()
    yT = nc.dram_tensor("yT", [H, S], F32, kind="ExternalOutput").ap()
    sck = "ExternalOutput" if dbg else "Internal"
    h_sc = nc.dram_tensor("h_sc", [II, S], BF16, kind=sck).ap()
    nh_sc = nc.dram_tensor("nh_sc", [II, S], BF16, kind=sck).ap()
    w2n_sc = nc.dram_tensor("w2n_sc", [NCH, ICH * 128 * 128], BF16, kind="Internal").ap()
    x1_sc = nc.dram_tensor("x1_sc", [H, S], BF16, kind=sck).ap()

    es = ExitStack()

    def sb(name, shp, dt_):
        return es.enter_context(nc.sbuf_tensor(name, shp, dt_))

    ABC = 83968
    AB = sb("AB", [128, ABC], BF16)
    ar = Arena(ABC)

    def ab(cols, e0, e1, name):
        off = ar.alloc(cols, e0, e1, name)
        return AB[:, off:off + cols]

    ln1T = ab(NCH * S, 0, 1, "ln1T")
    OT = ab(NH * S, 0, 4, "OT")
    xstr = [ab(S, 0, 0, f"xstr{i}") for i in range(2)]
    sqs0 = ab(S, 0, 0, "sqs0")
    Mb = ab(S, 0, 0, "Mb")
    Rb = ab(S, 0, 0, "Rb")
    t_s = ab(S, 1, 5, "t_s")
    d_s = ab(S, 1, 5, "d_s")
    e_s = ab(S, 1, 5, "e_s")
    z_s = ab(S, 1, 5, "z_s")
    wqh = [ab(4 * 384, 1, 1, f"wqh{i}") for i in range(2)]
    AQ = ab(S, 1, 1, "AQ")
    Vtm = ab(S, 1, 1, "Vtm")
    dd_s = ab(S, 1, 1, "dd_s")
    x1T = ab(NCH * S, 2, 3, "x1T")
    woutb = [ab(H, 2, 2, f"woutb{i}") for i in range(2)]
    xst3 = [ab(S, 2, 2, f"xst3{i}") for i in range(2)]
    Mb3 = ab(S, 3, 3, "Mb3")
    Rb3 = ab(S, 3, 3, "Rb3")
    gsc = ab(S, 3, 6, "gsc")
    sqs3 = gsc
    mxp = ab(S, 3, 5, "mxp")
    XS128 = ab(S, 3, 4, "XS128")
    RX128 = z_s
    w1s = ab(NCH * 1024, 4, 4, "w1s")
    WR128 = ab(1024, 4, 4, "WR128")
    gtile = ab(S, 4, 4, "gtile")
    utile = ab(S, 4, 4, "utile")
    htile = ab(S, 4, 5, "htile")
    hmx = ab(S, 4, 5, "hmx")
    HS128 = ab(S, 5, 6, "HS128")
    RH128 = ab(S, 5, 5, "RH128")
    w2sb = [ab(H, 5, 5, f"w2sb{i}") for i in range(2)]
    WR2c = ab(S, 5, 5, "WR2c")
    nhbuf = ab(ICH * 1024, 6, 6, "nhbuf")
    w2t = [ab(ICH * 128, 6, 6, f"w2t{i}") for i in range(2)]
    x1b = [ab(1024, 6, 6, f"x1b{i}") for i in range(2)]

    Qh = sb("Qh", [128, S], FP8)
    Kh = sb("Kh", [128, S], FP8)

    lam = sb("lam", [128, 1], F32)
    mu = sb("mu", [128, 1], F32)
    c49 = sb("c49", [128, 1], F32)
    onecolb = sb("onecolb", [128, 1], BF16)
    onerow = sb("onerow", [1, 128], F32)
    eyeb = sb("eyeb", [128, 128], BF16)
    boutS = sb("boutS", [128, NCH], F32)
    kons = sb("kons", [1, 8], F32)
    hsc = sb("hsc", [1, 32], F32)
    hbc = sb("hbc", [128, 12], F32)
    CC = sb("CC", [128, 8], F32)
    redc = sb("redc", [128, 1], F32)
    redr = sb("redr", [1, 128], F32)
    row0 = sb("row0", [1, S], F32)
    row1 = sb("row1", [1, S], F32)
    row2 = sb("row2", [1, S], F32)
    wscol = sb("wscol", [128, 16], F32)
    ws2col = sb("ws2col", [128, 16], F32)
    outf = sb("outf", [128, 1024], F32)
    konsP = sb("konsP", [128, 2], F32)

    RbF = sb("RbF", [128, S], F32)[:, :]
    psA = es.enter_context(nc.psum_tensor("psA", [128, S], F32))
    psO = es.enter_context(nc.psum_tensor("psO", [128, S], F32))

    sems = {k: es.enter_context(nc.semaphore("sm" + k)) for k in
            ["L", "G", "St", "V", "Sc", "T"]}
    em = Em(sems)

    def vfence(fn):
        val = em.v(fn, inc="V")
        em.w("vector", "V", val)
        return val

    def phase_barrier():
        for e in ("gpsimd", "sync", "tensor", "scalar", "vector"):
            for sem in ("T", "V", "Sc", "St", "G", "L"):
                em.w(e, sem, em.cnt[sem])

    def vTT(out, a, b_, op, inc=None):
        return em.v(lambda v: v.tensor_tensor(out, a, b_, op), inc)

    def vTS(out, a, s1, op, inc=None):
        return em.v(lambda v: v.tensor_single_scalar(out, a, s1, op), inc)

    def vTS2(out, a, s1, s2, op0, op1, inc=None):
        return em.v(lambda v: v.tensor_scalar(out, a, s1, s2, op0, op1), inc)

    def vSTT(out, in0, scal, in1, op0, op1, inc=None):
        return em.v(lambda v: v.scalar_tensor_tensor(out, in0, scal, in1, op0, op1), inc)

    def vcp(out, a, inc=None):
        return em.v(lambda v: v.tensor_copy(out, a), inc)

    def vms(ap_, val, inc=None):
        return em.v(lambda v: v.memset(ap_, val), inc)

    def int_round(out, src, F=S, inc=None):
        vTS2(t_s[:, 0:F], src, K23, K23, ALU.add, ALU.subtract)
        vTS2(d_s[:, 0:F], src, K52, K52, ALU.add, ALU.subtract)
        vTS(e_s[:, 0:F], t_s[:, 0:F], mu[:, 0:1], ALU.mult)
        return vSTT(out, d_s[:, 0:F], lam[:, 0:1], e_s[:, 0:F], ALU.mult, ALU.add, inc)

    def e4m3_blend(out, A, avw, inc=None):
        em.w("scalar", "V", avw)
        sv = em.s(lambda s: s.mul(t_s, A, c49[:, 0:1]), inc="Sc")
        em.w("vector", "Sc", sv)
        vTT(d_s, t_s, A, ALU.subtract)
        ev = vTT(e_s, t_s, d_s, ALU.subtract, inc="V")
        vTS2(z_s, A, K42, K42, ALU.add, ALU.subtract)
        em.w("scalar", "V", ev)
        sv = em.s(lambda s: s.activation(d_s, A, ACT.Square), inc="Sc")
        em.w("vector", "Sc", sv)
        vTS(d_s, d_s, float(2.0 ** -12), ALU.is_ge)
        vTT(e_s, e_s, z_s, ALU.subtract)
        vTT(e_s, e_s, d_s, ALU.mult)
        ev = vTT(e_s, e_s, z_s, ALU.add, inc="V")
        em.w("scalar", "V", ev)
        sv = em.s(lambda s: s.mul(t_s, A, mu[:, 0:1]), inc="Sc")
        em.w("vector", "Sc", sv)
        return vSTT(out, e_s, lam[:, 0:1], t_s, ALU.mult, ALU.add, inc)

    def ptree(src, F, op, dst_row):
        pta = d_s
        ptb = e_s
        em.w("gpsimd", "V", em.cnt["V"])
        cur = src
        w = 64
        while w >= 1:
            gv = em.dg(lambda g, cur=cur, w=w: g.dma_start(
                out=pta[0:w, 0:F], in_=cur[w:2 * w, 0:F]))
            em.w("vector", "G", gv)
            tv = vTT(ptb[0:w, 0:F], cur[0:w, 0:F], pta[0:w, 0:F], op, inc="V")
            em.w("gpsimd", "V", tv)
            cur = ptb
            w //= 2
        return vcp(dst_row[0:1, 0:F], ptb[0:1, 0:F], inc="V")

    def bcast(src_row, n, ps, dst):
        tv = None
        j = 0
        while j < n:
            nn = min(512, n - j)
            tv = em.t(lambda t, j=j, nn=nn: t.matmul(
                ps[:, j:j + nn], onerow[0:1, :], src_row[0:1, j:j + nn],
                start=True, stop=True), inc="T")
            j += nn
        em.w("vector", "T", tv)
        if n == 1:
            return vfence(lambda v: v.tensor_copy(dst[:, 0:1], ps[:, 0:1]))
        return vcp(dst[:, 0:n], ps[:, 0:n], inc="V")

    last_gather = [0]

    def amax_scalar(ps, col):
        em.w("vector", "G", last_gather[0])
        rv = em.v(lambda v: v.reduce_max(
            redc[:, 0:1], ps[:, :], axis=AXX, apply_absolute_value=True), inc="V")
        em.w("gpsimd", "V", rv)
        gv = em.dg(lambda g: g.dma_start(out=redr[0:1, 0:128], in_=redc[0:128, 0:1]))
        last_gather[0] = gv
        em.w("vector", "G", gv)
        return em.v(lambda v: v.reduce_max(
            hsc[0:1, col:col + 1], redr[0:1, :], axis=AXX), inc="V")

    def program():
        last_gather[0] = 0
        PH = PHASES
        p2n = P2N
        p2stop = P2STOP
        em.dg(lambda g: g.dma_start(out=eyeb[:, :], in_=eye))
        em.dl(lambda sy: sy.dma_start(out=boutS[:, :], in_=boutT))
        vfence(lambda v: v.memset(lam[:, :], 1.0))
        vfence(lambda v: v.tensor_scalar(lam[:, :], lam[:, :], float(2**30), float(2**30), ALU.add, ALU.subtract))
        vfence(lambda v: v.memset(mu[:, :], 1.0))
        vfence(lambda v: v.tensor_tensor(mu[:, :], mu[:, :], lam[:, :], ALU.subtract))
        vms(onecolb[:, :], 1.0)
        vms(onerow[:, :], 1.0)
        vms(konsP[:, 0:1], 1e-5)
        vms(konsP[:, 1:2], -SHIFT)
        vfence(lambda v: v.memset(CC[:, 3:4], 127.0))
        vfence(lambda v: v.reciprocal(CC[:, 0:1], CC[:, 3:4]))
        vfence(lambda v: v.memset(CC[:, 4:5], 224.0))
        vfence(lambda v: v.reciprocal(CC[:, 1:2], CC[:, 4:5]))
        vfence(lambda v: v.memset(CC[:, 3:4], 50176.0))
        vfence(lambda v: v.memset(CC[:, 5:6], SQ128))
        vfence(lambda v: v.memset(CC[:, 7:8], 1.0))
        vfence(lambda v: v.tensor_tensor(CC[:, 3:4], CC[:, 3:4], CC[:, 5:6], ALU.mult))
        vfence(lambda v: v.reciprocal(CC[:, 2:3], CC[:, 3:4]))
        vfence(lambda v: v.memset(c49[:, :], float(2**49)))
        vfence(lambda v: v.tensor_tensor(c49[:, :], c49[:, :], CC[:, 7:8], ALU.add))

        if PH < 2:
            em.w("gpsimd", "St", em.cnt["St"])
            return
        em.w("tensor", "V", em.cnt["V"])
        em.w("scalar", "V", em.cnt["V"])
        busy = [(0, 0), (0, 0)]
        for ci in range(NCH):
            b = ci % 2
            em.w("gpsimd", "T", busy[b][0])
            em.w("gpsimd", "Sc", busy[b][1])
            gv = em.dg(lambda g, ci=ci, b=b: g.dma_start(
                out=xstr[b], in_=xT[128 * ci:128 * (ci + 1), :]))
            em.w("tensor", "G", gv)
            tv = None
            for j in range(4):
                tv = em.t(lambda t, b=b, j=j, ci=ci: t.matmul(
                    psA[0:1, 512 * j:512 * (j + 1)], onecolb[:, :],
                    xstr[b][:, 512 * j:512 * (j + 1)],
                    start=(ci == 0), stop=(ci == NCH - 1)), inc="T")
            em.w("scalar", "G", gv)
            em.w("scalar", "T", tv)
            sv = em.s(lambda s, b=b: s.activation(sqs0, xstr[b], ACT.Square), inc="Sc")
            em.w("tensor", "Sc", sv)
            tv2 = None
            for j in range(4):
                tv2 = em.t(lambda t, j=j, ci=ci: t.matmul(
                    psO[0:1, 512 * j:512 * (j + 1)], onecolb[:, :],
                    sqs0[:, 512 * j:512 * (j + 1)],
                    start=(ci == 0), stop=(ci == NCH - 1)), inc="T")
            busy[b] = (tv2, sv)
        em.w("scalar", "T", em.cnt["T"])
        sv = em.s(lambda s: s.activation(row0[0:1, :], psA[0:1, :], ACT.Copy, scale=INV2048), inc="Sc")
        sv = em.s(lambda s: s.activation(row1[0:1, :], psO[0:1, :], ACT.Copy, scale=INV2048), inc="Sc")
        em.w("vector", "Sc", sv)
        vTT(row2[0:1, :], row0[0:1, :], row0[0:1, :], ALU.mult)
        ev = vTT(row1[0:1, :], row1[0:1, :], row2[0:1, :], ALU.subtract, inc="V")
        em.w("scalar", "V", ev)
        sv = em.s(lambda s: s.activation(row2[0:1, :], row1[0:1, :], ACT.Sqrt, bias=konsP[0:1, 0:1]), inc="Sc")
        em.w("vector", "Sc", sv)
        rv0 = em.v(lambda v: v.reciprocal(row1[0:1, :], row2[0:1, :]), inc="V")
        em.w("tensor", "Sc", sv)
        em.w("tensor", "V", rv0)
        bcast(row0[0:1, :], S, psA, Mb)
        bcast(row1[0:1, :], S, psO, RbF)

        if PH < 3:
            em.w("gpsimd", "St", em.cnt["St"])
            return
        em.w("gpsimd", "T", em.cnt["T"])
        em.w("gpsimd", "Sc", em.cnt["Sc"])
        vbusy = [0, 0]
        for ci in range(NCH):
            b = ci % 2
            em.w("gpsimd", "V", vbusy[b])
            gv = em.dg(lambda g, ci=ci, b=b: g.dma_start(
                out=xstr[b], in_=xT[128 * ci:128 * (ci + 1), :]))
            em.w("vector", "G", gv)
            vTT(sqs0, xstr[b], Mb, ALU.subtract)
            vbusy[b] = vTT(ln1T[:, S * ci:S * (ci + 1)], sqs0, RbF, ALU.mult, inc="V")
        p1v = em.cnt["V"]

        if PH < 4:
            em.w("gpsimd", "St", em.cnt["St"])
            return
        wq_busy = [0, 0]

        def qkv_mm(hh, ps, which):
            tv = None
            for grp in range(4):
                b = grp % 2
                em.w("gpsimd", "T", wq_busy[b])
                gv = em.dg(lambda g, hh=hh, grp=grp, b=b: g.dma_start(
                    out=wqh[b].rearrange("p (c f) -> p c f", f=384),
                    in_=wqkv.rearrange("(c p) f -> p c f", p=128)[
                        :, 4 * grp:4 * (grp + 1), 384 * hh:384 * (hh + 1)]))
                em.w("tensor", "G", gv)
                for c4 in range(4):
                    ci = 4 * grp + c4
                    for j in range(4):
                        tv = em.t(lambda t, b=b, c4=c4, ci=ci, j=j, ps=ps, which=which: t.matmul(
                            ps[:, 512 * j:512 * (j + 1)],
                            wqh[b][:, 384 * c4 + 128 * which:384 * c4 + 128 * (which + 1)],
                            ln1T[:, S * ci + 512 * j:S * ci + 512 * (j + 1)],
                            start=(ci == 0), stop=(ci == NCH - 1)), inc="T")
                wq_busy[b] = tv
            return tv

        em.w("tensor", "V", p1v)
        prev_evac = 0
        for hh in range(p2n):
            em.w("tensor", "V", prev_evac)
            tv = qkv_mm(hh, psA, 0)
            em.w("vector", "T", tv)
            if p2stop < 1:
                continue
            rv = amax_scalar(psA, 0)
            em.w("tensor", "V", rv)
            bcast(hsc[0:1, 0:1], 1, psO, hbc[:, 0:1])
            bq = em.cnt["V"]
            vfence(lambda v: v.reciprocal(hbc[:, 3:4], hbc[:, 0:1]))
            vfence(lambda v: v.tensor_tensor(hbc[:, 4:5], hbc[:, 3:4], CC[:, 4:5], ALU.mult))
            av = vTS(AQ, psA[:, :], hbc[:, 4:5], ALU.mult, inc="V")
            qv = e4m3_blend(Qh[:, :], AQ, av, inc="V")
            if p2stop < 2:
                continue
            em.w("tensor", "V", bq)
            tv = qkv_mm(hh, psO, 1)
            em.w("vector", "T", tv)
            rv = amax_scalar(psO, 1)
            em.w("tensor", "V", max(rv, av))
            bcast(hsc[0:1, 1:2], 1, psA, hbc[:, 1:2])
            bk = em.cnt["V"]
            vfence(lambda v: v.reciprocal(hbc[:, 3:4], hbc[:, 1:2]))
            vfence(lambda v: v.tensor_tensor(hbc[:, 5:6], hbc[:, 3:4], CC[:, 4:5], ALU.mult))
            av = vTS(AQ, psO[:, :], hbc[:, 5:6], ALU.mult, inc="V")
            kv = e4m3_blend(Kh[:, :], AQ, av, inc="V")
            em.w("tensor", "V", bk)
            tv = qkv_mm(hh, psA, 2)
            em.w("vector", "T", tv)
            if p2stop < 3:
                continue
            rv = amax_scalar(psA, 2)
            em.w("tensor", "V", max(rv, av))
            bcast(hsc[0:1, 2:3], 1, psO, hbc[:, 2:3])
            vfence(lambda v: v.reciprocal(hbc[:, 3:4], hbc[:, 2:3]))
            vfence(lambda v: v.tensor_tensor(hbc[:, 6:7], hbc[:, 3:4], CC[:, 4:5], ALU.mult))
            vfence(lambda v: v.tensor_tensor(hbc[:, 7:8], hbc[:, 2:3], CC[:, 1:2], ALU.mult))
            vfence(lambda v: v.tensor_tensor(hbc[:, 3:4], hbc[:, 0:1], hbc[:, 1:2], ALU.mult))
            vfence(lambda v: v.tensor_tensor(hbc[:, 8:9], hbc[:, 3:4], CC[:, 2:3], ALU.mult))
            av = vTS(AQ, psA[:, :], hbc[:, 6:7], ALU.mult, inc="V")
            vv8 = e4m3_blend(z_s, AQ, av, inc="V")
            if p2stop < 4:
                continue
            em.w("tensor", "V", vv8)
            tv = None
            for kt in range(NCH):
                tv = em.t(lambda t, kt=kt: t.matmul(
                    psO[:, 128 * kt:128 * (kt + 1)],
                    z_s[:, 128 * kt:128 * (kt + 1)], eyeb[:, :],
                    start=True, stop=True), inc="T")
            em.w("vector", "T", tv)
            vcp(Vtm, psO[:, :])
            vms(dd_s, 0.0, inc="V")
            em.w("tensor", "V", em.cnt["V"])
            if p2stop < 5:
                continue
            pv_t = None
            dv = None
            for kt in range(NCH):
                tv = None
                for j in range(4):
                    tv = em.t(lambda t, kt=kt, j=j: t.matmul(
                        psA[:, 512 * j:512 * (j + 1)],
                        Kh[:, 128 * kt:128 * (kt + 1)],
                        Qh[:, 512 * j:512 * (j + 1)],
                        start=True, stop=True), inc="T")
                em.w("scalar", "T", tv)
                if dv is not None:
                    em.w("scalar", "V", dv)
                if pv_t is not None:
                    em.w("scalar", "T", pv_t)
                sv = em.s(lambda s: s.activation(
                    AQ, psA[:, :], ACT.Exp, bias=konsP[:, 1:2], scale=hbc[:, 8:9]), inc="Sc")
                em.w("vector", "Sc", sv)
                dv = vTT(dd_s, dd_s, AQ, ALU.add, inc="V")
                em.w("tensor", "Sc", sv)
                for j in range(4):
                    pv_t = em.t(lambda t, kt=kt, j=j: t.matmul(
                        psO[:, 512 * j:512 * (j + 1)],
                        Vtm[:, 128 * kt:128 * (kt + 1)],
                        AQ[:, 512 * j:512 * (j + 1)],
                        start=(kt == 0), stop=(kt == NCH - 1)), inc="T")
            if p2stop < 6:
                continue
            rv = ptree(dd_s, S, ALU.add, row2)
            em.w("tensor", "V", rv)
            em.w("tensor", "Sc", em.cnt["Sc"])
            bcast(row2[0:1, :], S, psA, dd_s)
            em.v(lambda v: v.reciprocal(AQ, dd_s))
            vTS(AQ, AQ, hbc[:, 7:8], ALU.mult, inc="V")
            em.w("vector", "T", pv_t)
            prev_evac = vTT(OT[:, S * hh:S * (hh + 1)], psO[:, :], AQ, ALU.mult, inc="V")
            phase_barrier()
        p2v = em.cnt["V"]
        if dbg:
            em.w("gpsimd", "V", p2v)
            for ci in range(NCH):
                em.dst(lambda g, ci=ci: g.dma_start(
                    out=h_sc[128 * ci:128 * (ci + 1), :], in_=OT[:, S * ci:S * (ci + 1)]))
            em.dst(lambda g: g.dma_start(out=nh_sc[0:128, :], in_=Qh[:, :]))
            em.dst(lambda g: g.dma_start(out=nh_sc[128:256, :], in_=Kh[:, :]))
            em.dst(lambda g: g.dma_start(out=nh_sc[256:384, :], in_=Vtm[:, :]))
            em.dst(lambda g: g.dma_start(out=nh_sc[384:512, :], in_=dd_s[:, :]))
            em.dst(lambda g: g.dma_start(out=nh_sc[512:640, 0:12], in_=hbc[:, 0:12]))
            em.dst(lambda g: g.dma_start(out=nh_sc[640:768, 0:4], in_=CC[:, 0:4]))
            em.dst(lambda g: g.dma_start(out=nh_sc[768:896, :], in_=AQ[:, :]))
            em.dst(lambda g: g.dma_start(out=nh_sc[896:1024, :], in_=ln1T[:, 0:S]))
            em.dst(lambda g: g.dma_start(out=nh_sc[1024:1152, :], in_=Mb[:, :]))
            em.dst(lambda g: g.dma_start(out=nh_sc[1152:1280, :], in_=Rb[:, :]))

        if PH < 5:
            em.w("gpsimd", "St", em.cnt["St"])
            return
        phase_barrier()
        wo_busy = [0, 0]
        x3_busy = [0, 0]

        def load_wo(ot, b):
            em.w("gpsimd", "T", wo_busy[b])
            gv = None
            for g2 in range(2):
                gv = em.dg(lambda g, ot=ot, b=b, g2=g2: g.dma_start(
                    out=woutb[b][:, 1024 * g2:1024 * (g2 + 1)].rearrange(
                        "p (c f) -> p c f", f=128),
                    in_=wout.rearrange("(c p) f -> p c f", p=128)[
                        :, 8 * g2:8 * (g2 + 1), 128 * ot:128 * (ot + 1)]))
            return gv

        wol = [0, 0]
        wol[0] = load_wo(0, 0)
        evp = p2v
        for ot in range(NCH):
            b = ot % 2
            if ot + 1 < NCH:
                wol[1 - b] = load_wo(ot + 1, 1 - b)
            em.w("gpsimd", "V", x3_busy[b])
            xl = em.dg(lambda g, ot=ot, b=b: g.dma_start(
                out=xst3[b], in_=xT[128 * ot:128 * (ot + 1), :]))
            em.w("tensor", "G", wol[b])
            em.w("tensor", "V", evp)
            tv = None
            for ci in range(NCH):
                for j in range(4):
                    tv = em.t(lambda t, ci=ci, j=j, b=b: t.matmul(
                        psA[:, 512 * j:512 * (j + 1)],
                        woutb[b][:, 128 * ci:128 * (ci + 1)],
                        OT[:, S * ci + 512 * j:S * ci + 512 * (j + 1)],
                        start=(ci == 0), stop=(ci == NCH - 1)), inc="T")
            wo_busy[b] = tv
            em.w("vector", "T", tv)
            em.w("vector", "G", xl)
            evp = vSTT(x1T[:, S * ot:S * (ot + 1)], psA[:, :], boutS[:, ot:ot + 1],
                       xst3[b], ALU.add, ALU.add, inc="V")
            x3_busy[b] = evp
            em.w("gpsimd", "V", evp)
            em.dst(lambda g, ot=ot: g.dma_start(
                out=x1_sc[128 * ot:128 * (ot + 1), :], in_=x1T[:, S * ot:S * (ot + 1)]))
            phase_barrier()

        if PH < 6:
            em.w("gpsimd", "St", em.cnt["St"])
            return
        phase_barrier()
        if dbg:
            em.dst(lambda g: g.dma_start(
                out=h_sc[2048:2176, :], in_=x1T[:, S * 15:S * 16]))
        for ci in range(NCH):
            tv = None
            for j in range(4):
                tv = em.t(lambda t, ci=ci, j=j: t.matmul(
                    psA[0:1, 512 * j:512 * (j + 1)], onecolb[:, :],
                    x1T[:, S * ci + 512 * j:S * ci + 512 * (j + 1)],
                    start=(ci == 0), stop=(ci == NCH - 1)), inc="T")
            em.w("scalar", "T", tv)
            sv = em.s(lambda s, ci=ci: s.activation(
                sqs3, x1T[:, S * ci:S * (ci + 1)], ACT.Square), inc="Sc")
            em.w("tensor", "Sc", sv)
            for j in range(4):
                em.t(lambda t, j=j, ci=ci: t.matmul(
                    psO[0:1, 512 * j:512 * (j + 1)], onecolb[:, :],
                    sqs3[:, 512 * j:512 * (j + 1)],
                    start=(ci == 0), stop=(ci == NCH - 1)), inc="T")
        em.w("scalar", "T", em.cnt["T"])
        sv = em.s(lambda s: s.activation(row0[0:1, :], psA[0:1, :], ACT.Copy, scale=INV2048), inc="Sc")
        sv = em.s(lambda s: s.activation(row1[0:1, :], psO[0:1, :], ACT.Copy, scale=INV2048), inc="Sc")
        em.w("vector", "Sc", sv)
        vTT(row2[0:1, :], row0[0:1, :], row0[0:1, :], ALU.mult)
        ev = vTT(row1[0:1, :], row1[0:1, :], row2[0:1, :], ALU.subtract, inc="V")
        em.w("scalar", "V", ev)
        sv = em.s(lambda s: s.activation(row2[0:1, :], row1[0:1, :], ACT.Sqrt, bias=konsP[0:1, 0:1]), inc="Sc")
        em.w("vector", "Sc", sv)
        rv0 = em.v(lambda v: v.reciprocal(row1[0:1, :], row2[0:1, :]), inc="V")
        em.w("tensor", "Sc", sv)
        em.w("tensor", "V", rv0)
        bcast(row0[0:1, :], S, psA, Mb3)
        bcast(row1[0:1, :], S, psO, RbF)
        phase_barrier()
        vms(mxp, BIGNEG)
        for ci in range(NCH):
            vTT(sqs3, x1T[:, S * ci:S * (ci + 1)], Mb3, ALU.subtract)
            vTT(sqs3, sqs3, RbF, ALU.mult)
            vTT(mxp, mxp, sqs3, ALU.max)
            vTS(d_s, sqs3, -1.0, ALU.mult)
            vTT(mxp, mxp, d_s, ALU.max, inc="V")
        rv = ptree(mxp, S, ALU.max, row2)
        em.w("tensor", "V", rv)
        bcast(row2[0:1, :], S, psA, XS128)
        vTS(XS128, XS128, CC[:, 0:1], ALU.mult)
        vTS(XS128, XS128, 1e-8, ALU.max)
        em.v(lambda v: v.reciprocal(RX128, XS128), inc="V")
        phase_barrier()
        em.w("vector", "T", em.cnt["T"])
        for ci in range(NCH):
            vTT(sqs3, x1T[:, S * ci:S * (ci + 1)], Mb3, ALU.subtract)
            vTT(sqs3, sqs3, RbF, ALU.mult)
            vTT(sqs3, sqs3, RX128, ALU.mult)
            int_round(OT[:, S * ci:S * (ci + 1)], sqs3, inc="V")

        if dbg:
            em.w("gpsimd", "V", em.cnt["V"])
            em.w("gpsimd", "Sc", em.cnt["Sc"])
            em.dst(lambda g: g.dma_start(
                out=h_sc[2176:2304, :], in_=x1T[:, S * 15:S * 16]))
            for ci in range(NCH):
                em.dst(lambda g, ci=ci: g.dma_start(
                    out=h_sc[2560 + 128 * ci:2688 + 128 * ci, :],
                    in_=OT[:, S * ci:S * (ci + 1)]))
            em.dst(lambda g: g.dma_start(out=h_sc[4608:4736, :], in_=XS128[:, :]))
        if PH < 7:
            em.w("gpsimd", "St", em.cnt["St"])
            return
        phase_barrier()
        vms(hmx, BIGNEG, inc="V")
        for sl in range(NSL):
            phase_barrier()
            em.w("vector", "St", em.cnt["St"])
            em.w("gpsimd", "T", em.cnt["T"])
            gv = None
            for grp in range(4):
                gv = em.dg(lambda g, sl=sl, grp=grp: g.dma_start(
                    out=w1s[:, 4096 * grp:4096 * (grp + 1)].rearrange(
                        "p (c f) -> p c f", f=1024),
                    in_=wguT.rearrange("(c p) f -> p c f", p=128)[
                        :, 4 * grp:4 * (grp + 1), 1024 * sl:1024 * (sl + 1)]))
            em.w("vector", "G", gv)
            if dbg and sl == 1 and PH < 0:
                em.w("gpsimd", "G", gv)
                for ci4 in range(8):
                    em.dst(lambda g, ci4=ci4: g.dma_start(
                        out=yT[769 + 128 * ci4:897 + 128 * ci4, 0:1024],
                        in_=w1s[:, 1024 * (ci4 + 8):1024 * (ci4 + 9)]))
                em.w("vector", "St", em.cnt["St"])
            vfence(lambda v: v.memset(mxp[:, 0:1024], BIGNEG))
            for ci in range(NCH):
                vfence(lambda v: v.tensor_tensor(mxp[:, 0:1024], mxp[:, 0:1024], w1s[:, 1024 * ci:1024 * (ci + 1)], op=ALU.max))
                vfence(lambda v: v.tensor_single_scalar(d_s[:, 0:1024], w1s[:, 1024 * ci:1024 * (ci + 1)], -1.0, op=ALU.mult))
                vfence(lambda v: v.tensor_tensor(mxp[:, 0:1024], mxp[:, 0:1024], d_s[:, 0:1024], op=ALU.max))
            if dbg and sl == 1 and PH < 0:
                em.w("gpsimd", "V", em.cnt["V"])
                em.dst(lambda g: g.dma_start(out=yT[640:768, 0:1024], in_=mxp[:, 0:1024]))
                em.w("vector", "St", em.cnt["St"])
            rv = ptree(mxp[:, 0:1024], 1024, ALU.max, row2)
            if dbg and sl == 1 and PH < 0:
                em.w("gpsimd", "V", em.cnt["V"])
                em.dst(lambda g: g.dma_start(out=yT[768:769, 0:1024], in_=row2[0:1, 0:1024]))
                em.w("vector", "St", em.cnt["St"])
            em.w("tensor", "V", rv)
            bcast(row2[0:1, 0:1024], 1024, psA, WR128)
            vTS(WR128[:, 0:1024], WR128[:, 0:1024], CC[:, 0:1], ALU.mult)
            rv = vTS(WR128[:, 0:1024], WR128[:, 0:1024], 1e-8, ALU.max, inc="V")
            em.v(lambda v: v.reciprocal(mxp[:, 0:1024], WR128[:, 0:1024]), inc="V")
            if dbg and sl == 1 and PH < 0:
                em.w("gpsimd", "V", em.cnt["V"])
                em.dst(lambda g: g.dma_start(out=yT[0:128, 0:1024], in_=WR128[:, 0:1024]))
                em.w("vector", "St", em.cnt["St"])
            em.w("gpsimd", "V", rv)
            gv2 = None
            for tt_ in range(8):
                gv2 = em.dg(lambda g, tt_=tt_: g.dma_start(
                    out=wscol[:, tt_:tt_ + 1], in_=WR128[0:1, 128 * tt_:128 * (tt_ + 1)]))
            em.w("vector", "G", gv2)
            for ci in range(NCH):
                vTT(gsc[:, 0:1024], w1s[:, 1024 * ci:1024 * (ci + 1)], mxp[:, 0:1024], ALU.mult)
                int_round(w1s[:, 1024 * ci:1024 * (ci + 1)], gsc[:, 0:1024], F=1024, inc="V")
            phase_barrier()
            wq_done = em.cnt["V"]
            if dbg and sl == 1 and PH < 0:
                em.w("gpsimd", "V", wq_done)
                em.dst(lambda g: g.dma_start(out=yT[128:256, 0:16], in_=wscol[:, 0:16]))
                em.dst(lambda g: g.dma_start(out=yT[256:384, 0:1024], in_=WR128[:, 0:1024]))
                em.dst(lambda g: g.dma_start(out=yT[384:512, 0:1024], in_=mxp[:, 0:1024]))
                em.dst(lambda g: g.dma_start(out=yT[512:640, 0:1024], in_=w1s[:, 0:1024]))
                em.w("vector", "St", em.cnt["St"])
            em.w("tensor", "V", wq_done)
            for p4 in range(4):
                tv = None
                for ci in range(NCH):
                    for j in range(4):
                        tv = em.t(lambda t, ci=ci, j=j, p4=p4: t.matmul(
                            psA[:, 512 * j:512 * (j + 1)],
                            w1s[:, 1024 * ci + 128 * p4:1024 * ci + 128 * (p4 + 1)],
                            OT[:, S * ci + 512 * j:S * ci + 512 * (j + 1)],
                            start=(ci == 0), stop=(ci == NCH - 1)), inc="T")
                for ci in range(NCH):
                    for j in range(4):
                        tv = em.t(lambda t, ci=ci, j=j, p4=p4: t.matmul(
                            psO[:, 512 * j:512 * (j + 1)],
                            w1s[:, 1024 * ci + 512 + 128 * p4:1024 * ci + 512 + 128 * (p4 + 1)],
                            OT[:, S * ci + 512 * j:S * ci + 512 * (j + 1)],
                            start=(ci == 0), stop=(ci == NCH - 1)), inc="T")
                em.w("vector", "T", tv)
                vSTT(gtile, psA[:, :], wscol[:, p4:p4 + 1], XS128, ALU.mult, ALU.mult)
                ev = vSTT(utile, psO[:, :], wscol[:, 4 + p4:5 + p4], XS128, ALU.mult, ALU.mult, inc="V")
                em.w("scalar", "V", ev)
                sv = em.s(lambda s: s.activation(gsc, gtile, ACT.Silu), inc="Sc")
                em.w("vector", "Sc", sv)
                em.w("vector", "St", em.cnt["St"])
                vTT(htile, gsc, utile, ALU.mult)
                vTT(hmx, hmx, htile, ALU.max)
                vTS(d_s, htile, -1.0, ALU.mult)
                hv = vTT(hmx, hmx, d_s, ALU.max, inc="V")
                em.w("gpsimd", "V", hv)
                em.dst(lambda g, sl=sl, p4=p4: g.dma_start(
                    out=h_sc[512 * sl + 128 * p4:512 * sl + 128 * (p4 + 1), :],
                    in_=htile))
                em.w("tensor", "V", hv)

        if PH < 8:
            em.w("gpsimd", "St", em.cnt["St"])
            return
        phase_barrier()
        rv = ptree(hmx, S, ALU.max, row2)
        em.w("tensor", "V", rv)
        bcast(row2[0:1, :], S, psA, HS128)
        vTS(HS128, HS128, CC[:, 0:1], ALU.mult)
        vTS(HS128, HS128, 1e-8, ALU.max)
        em.v(lambda v: v.reciprocal(RH128, HS128), inc="V")
        for ic in range(ICH):
            em.w("sync", "St", em.cnt["St"])
            em.w("sync", "V", em.cnt["V"])
            hql = em.dl(lambda sy, ic=ic: sy.dma_start(
                out=htile, in_=h_sc[128 * ic:128 * (ic + 1), :]))
            em.w("vector", "L", hql)
            vTT(gsc, htile, RH128, ALU.mult)
            int_round(htile, gsc)
            hv = vTS2(htile, htile, 127.0, -127.0, ALU.min, ALU.max, inc="V")
            em.w("gpsimd", "V", hv)
            em.dst(lambda g, ic=ic: g.dma_start(
                out=nh_sc[128 * ic:128 * (ic + 1), :], in_=htile))
        phase_barrier()
        vms(mxp, BIGNEG, inc="V")
        w2_busy = [0, 0]
        for ic in range(ICH):
            b = ic % 2
            em.w("gpsimd", "V", w2_busy[b])
            gv = em.dg(lambda g, ic=ic, b=b: g.dma_start(
                out=w2sb[b], in_=wdnT[128 * ic:128 * (ic + 1), :]))
            em.w("vector", "G", gv)
            vTT(mxp, mxp, w2sb[b], ALU.max)
            vTS(d_s, w2sb[b], -1.0, ALU.mult)
            w2_busy[b] = vTT(mxp, mxp, d_s, ALU.max, inc="V")
        rv = ptree(mxp, S, ALU.max, row2)
        em.w("tensor", "V", rv)
        bcast(row2[0:1, :], S, psA, WR2c)
        vTS(WR2c, WR2c, CC[:, 0:1], ALU.mult)
        rv = vTS(WR2c, WR2c, 1e-8, ALU.max, inc="V")
        em.v(lambda v: v.reciprocal(mxp, WR2c), inc="V")
        em.w("gpsimd", "V", rv)
        gv2 = None
        for tt_ in range(16):
            gv2 = em.dg(lambda g, tt_=tt_: g.dma_start(
                out=ws2col[:, tt_:tt_ + 1], in_=WR2c[0:1, 128 * tt_:128 * (tt_ + 1)]))
        em.w("vector", "G", gv2)
        w2_busy = [0, 0]
        for ic in range(ICH):
            b = ic % 2
            em.w("gpsimd", "V", w2_busy[b])
            gv = em.dg(lambda g, ic=ic, b=b: g.dma_start(
                out=w2sb[b], in_=wdnT[128 * ic:128 * (ic + 1), :]))
            em.w("vector", "G", gv)
            vTT(gsc, w2sb[b], mxp, ALU.mult)
            wv = int_round(w2sb[b], gsc, inc="V")
            w2_busy[b] = wv
            em.w("gpsimd", "V", wv)
            em.dst(lambda g, ic=ic, b=b: g.dma_start(
                out=w2n_sc[:, 16384 * ic:16384 * (ic + 1)].rearrange(
                    "t (p o) -> p t o", o=128),
                in_=w2sb[b].rearrange("p (t o) -> p t o", o=128)))
        phase_barrier()
        w2done_st = em.cnt["St"]

        if PH < 9:
            em.w("gpsimd", "St", em.cnt["St"])
            return
        phase_barrier()
        for half in range(2):
            em.w("sync", "T", em.cnt["T"])
            em.w("sync", "St", w2done_st)
            nl = 0
            for grp in range(4):
                n0 = 11 * grp
                nl = em.dl(lambda sy, n0=n0, half=half: sy.dma_start(
                    out=nhbuf[:, 1024 * n0:1024 * (n0 + 11)].rearrange(
                        "p (c s) -> p c s", s=1024),
                    in_=nh_sc.rearrange("(c p) s -> p c s", p=128)[
                        :, n0:n0 + 11, 1024 * half:1024 * (half + 1)]))
            w2t_busy = [0, 0]
            x1_busy = [0, 0]
            stv = em.cnt["St"]
            w2tl = [0, 0]
            w2tl[0] = em.dl(lambda sy: sy.dma_start(
                out=w2t[0].rearrange("p (i o) -> p i o", o=128),
                in_=w2n_sc[0:1, :].rearrange("a (i p o) -> (a p) i o", p=128, o=128)))
            evq = em.cnt["V"]
            for ot in range(NCH):
                b = ot % 2
                if ot + 1 < NCH:
                    em.w("sync", "T", w2t_busy[1 - b])
                    w2tl[1 - b] = em.dl(lambda sy, ot=ot: sy.dma_start(
                        out=w2t[1 - (ot % 2)].rearrange("p (i o) -> p i o", o=128),
                        in_=w2n_sc[ot + 1:ot + 2, :].rearrange(
                            "a (i p o) -> (a p) i o", p=128, o=128)))
                em.w("sync", "V", x1_busy[b])
                xlv = em.dl(lambda sy, ot=ot, half=half, b=b: sy.dma_start(
                    out=x1b[b], in_=x1_sc[128 * ot:128 * (ot + 1),
                                          1024 * half:1024 * (half + 1)]))
                em.w("tensor", "L", max(w2tl[b], nl, xlv))
                em.w("tensor", "V", evq)
                pcols = (ot % 2) * 1024
                tv = None
                for ic in range(ICH):
                    for j in range(2):
                        tv = em.t(lambda t, ic=ic, j=j, b=b, pcols=pcols: t.matmul(
                            psA[:, pcols + 512 * j:pcols + 512 * (j + 1)],
                            w2t[b][:, 128 * ic:128 * (ic + 1)],
                            nhbuf[:, 1024 * ic + 512 * j:1024 * ic + 512 * (j + 1)],
                            start=(ic == 0), stop=(ic == ICH - 1)), inc="T")
                w2t_busy[b] = tv
                em.w("vector", "T", tv)
                em.w("vector", "St", stv)
                vSTT(outf[:, :], psA[:, pcols:pcols + 1024], ws2col[:, ot:ot + 1],
                     HS128[:, 1024 * half:1024 * (half + 1)], ALU.mult, ALU.mult)
                evq = vTT(outf[:, :], outf[:, :], x1b[b], ALU.add, inc="V")
                x1_busy[b] = evq
                em.w("gpsimd", "V", evq)
                stv = em.dst(lambda g, ot=ot, half=half: g.dma_start(
                    out=yT[128 * ot:128 * (ot + 1), 1024 * half:1024 * (half + 1)],
                    in_=outf[:, :]))
        em.w("gpsimd", "St", em.cnt["St"])

    with es:
        with nc.Block() as block:
            @block.sync
            def _(sy):
                em.start("sync", sy)
                with nc.allow_low_precision(reason="bf16 intermediates; replay-exact"):
                    program()

            @block.gpsimd
            def _(g):
                em.start("gpsimd", g)
                with nc.allow_low_precision(reason="bf16 intermediates; replay-exact"):
                    program()

            @block.vector
            def _(v):
                em.start("vector", v)
                with nc.allow_low_precision(reason="bf16 intermediates; replay-exact"):
                    program()

            @block.scalar
            def _(s):
                em.start("scalar", s)
                with nc.allow_low_precision(reason="bf16 intermediates; replay-exact"):
                    program()

            @block.tensor
            def _(t):
                em.start("tensor", t)
                with nc.allow_low_precision(reason="bf16 intermediates; replay-exact"):
                    program()

    return nc


_NC_CACHE = {}


def _get_nc(dbg=False, phases=9, p2n=16, p2stop=9):
    key = (dbg, phases, p2n, p2stop)
    if key not in _NC_CACHE:
        _NC_CACHE[key] = build_nc(dbg, phases, p2n, p2stop)
    return _NC_CACHE[key]


def kernel(L0_attn_x, L0_attn_W_qkv, L0_attn_W_out, L0_attn_b_out,
           L0_ffn_w_gate_up, L0_ffn_w_down, _dbg=False, _phases=9, _p2n=16, _p2stop=9):
    x = np.ascontiguousarray(np.asarray(L0_attn_x, dtype=np.float32))
    Wq = np.ascontiguousarray(
        np.asarray(L0_attn_W_qkv, dtype=np.float32).reshape(H, 3, NH, 128)
        .transpose(0, 2, 1, 3).reshape(H, 3 * H))
    Wo = np.ascontiguousarray(np.asarray(L0_attn_W_out, dtype=np.float32))
    bo = np.ascontiguousarray(
        np.asarray(L0_attn_b_out, dtype=np.float32).reshape(NCH, 128).T)
    WguT = np.asarray(L0_ffn_w_gate_up, dtype=np.float32).T
    perm = np.concatenate([np.concatenate(
        [np.arange(512 * s_, 512 * (s_ + 1)), II + np.arange(512 * s_, 512 * (s_ + 1))])
        for s_ in range(NSL)])
    WguT = np.ascontiguousarray(WguT[:, perm])
    WdnT = np.ascontiguousarray(np.asarray(L0_ffn_w_down, dtype=np.float32).T)
    eyem = np.eye(128, dtype=np.float32)

    nc = _get_nc(_dbg, _phases, _p2n, _p2stop)
    in_maps = []
    for bb in range(B):
        in_maps.append({
            "xT": np.ascontiguousarray(x[bb].T),
            "wqkv": Wq, "wout": Wo, "boutT": bo,
            "wguT": WguT, "wdnT": WdnT, "eye": eyem,
        })
    res = run_bass_kernel_spmd(nc, in_maps, core_ids=list(range(B)))
    if _dbg:
        return res
    out = np.empty((B, S, H), dtype=np.float32)
    for bb in range(B):
        out[bb] = res.results[bb]["yT"].T
    return out
